# version 14
# baseline (speedup 1.0000x reference)
import numpy as np
from contextlib import ExitStack

import concourse.bass as bass
import concourse.mybir as mybir
from concourse.bass_utils import run_bass_kernel_spmd

F32 = mybir.dt.float32
BF16 = mybir.dt.bfloat16
U8 = mybir.dt.uint8
AF = mybir.ActivationFunctionType
OP = mybir.AluOpType
AX = mybir.AxisListType.X

T = 4096
D = 4096
H = 4
HD = 1024
QPC = 1024
NQT = 2
NST = 32
HIDP = 11264
NHC = 88
CFP = 2816
NCF = 22
NSLAB = 22
NW2SL = 8
SGRP = (16, 32)
EPS = 1e-5
KB = 1024

AO = dict(xbuf0=96*KB, xbuf1=112*KB, xlnb0=144*KB, xlnb1=152*KB, trashV=128*KB,
          trashS=160*KB, statc=202*KB, bq=177*KB, bk=181*KB, qk4=185*KB,
          vv4=189*KB, sg1=193*KB, sg2=197*KB)
BO = dict(kfm=0, vtm=64*KB, strip=128*KB, qh=160*KB, xq4=176*KB,
          ystg=184*KB, praw=192*KB, msk=194*KB)
CO = dict(wsl1=0, wsl3=16*KB, xf=32*KB, hf=48*KB, xn2=92*KB, ybuf=100*KB,
          ybf=108*KB, ysq=112*KB, silu01=114*KB, htmp=116*KB,
          b1aug=118*KB, b3aug=140*KB, b2aug=162*KB,
          z1bf=170*KB, z3bf=172*KB, z2bf=174*KB,
          rows=176*KB, rowsb=192*KB, r2b=196*KB, m2rb=198*KB,
          w2sl=0, ostg=92*KB)


def build_nc(debug=False):
    nc = bass.Bass("TRN2", target_bir_lowering=False)

    x_in = nc.dram_tensor("x", [T, D], F32, kind="ExternalInput").ap()
    xqT = nc.dram_tensor("xqT", [D, QPC], F32, kind="ExternalInput").ap()
    thr = nc.dram_tensor("thr", [128, 96], F32, kind="ExternalInput").ap()
    iota = nc.dram_tensor("iota", [128, 512], F32, kind="ExternalInput").ap()
    ident = nc.dram_tensor("ident", [128, 128], F32, kind="ExternalInput").ap()
    bvq = nc.dram_tensor("bvq", [H, HD], F32, kind="ExternalInput").ap()
    bvk = nc.dram_tensor("bvk", [H, HD], F32, kind="ExternalInput").ap()
    bvv = nc.dram_tensor("bvv", [H, HD], F32, kind="ExternalInput").ap()
    w1p = nc.dram_tensor("w1p", [128, NSLAB, 8, 512], F32, kind="ExternalInput").ap()
    w3p = nc.dram_tensor("w3p", [128, NSLAB, 8, 512], F32, kind="ExternalInput").ap()
    w2p = nc.dram_tensor("w2p", [128, NW2SL, NCF, 512], F32, kind="ExternalInput").ap()
    a13p = nc.dram_tensor("a13p", [128, 32, 4], F32, kind="ExternalInput").ap()
    a2p = nc.dram_tensor("a2p", [128, NHC, 2], F32, kind="ExternalInput").ap()
    b1T = nc.dram_tensor("b1T", [2, HIDP], F32, kind="ExternalInput").ap()
    b3T = nc.dram_tensor("b3T", [2, HIDP], F32, kind="ExternalInput").ap()
    b2T = nc.dram_tensor("b2T", [2, D], F32, kind="ExternalInput").ap()

    out_d = nc.dram_tensor("out", [D, QPC], F32, kind="ExternalOutput").ap()
    xln_s = nc.dram_tensor("xln_s", [T, D], BF16,
                           kind="ExternalOutput" if debug else "Internal").ap()
    y_s = nc.dram_tensor("y_s", [D, QPC], F32,
                         kind="ExternalOutput" if debug else "Internal").ap()
    xfd = (nc.dram_tensor("xfd", [128, 8 * KB], BF16, kind="ExternalOutput").ap()
           if debug else None)
    hfd = (nc.dram_tensor("hfd", [128, NCF * KB], BF16, kind="ExternalOutput").ap()
           if debug else None)
    zd = (nc.dram_tensor("zd", [2, 3 * KB], BF16, kind="ExternalOutput").ap()
          if debug else None)

    with ExitStack() as ctx:
        def sb(name, shape, dtype):
            return nc.alloc_sbuf_tensor(name, shape, dtype)
        iotat = sb("iotat", [128, 512], F32)
        thrt = sb("thrt", [128, 96], F32)
        identt = sb("identt", [128, 128], F32)
        qkcol = sb("qkcol", [128, 8, 4], F32)
        vcolt = sb("vcolt", [128, 8, 4], F32)
        a13sb = sb("a13sb", [128, 32, 4], BF16)
        a2sb = sb("a2sb", [128, NHC, 2], BF16)
        onescol = sb("onescol", [128, 1], BF16)
        onesrow = sb("onesrow", [1, 128], BF16)
        g0w = sb("g0w", [128, 33], BF16)
        g1w = sb("g1w", [128, 33], BF16)

        arena = nc.alloc_sbuf_tensor("arena", [128, 203 * KB], U8)
        abase = None
        for alloc in nc.m.functions[0].allocations:
            try:
                ml = alloc.memorylocations[0]
                if ml.name == "arena":
                    abase = ml.addr
            except Exception:
                pass
        assert abase is not None

        def at(name, shape, dtype, off):
            return nc.alloc_sbuf_tensor_at(name, shape, dtype, offset=abase + off)

        xbuf = [at("xbuf0", [128, D], F32, AO["xbuf0"]),
                at("xbuf1", [128, D], F32, AO["xbuf1"])]
        xlnb = [at("xlnb0", [128, D], BF16, AO["xlnb0"]),
                at("xlnb1", [128, D], BF16, AO["xlnb1"])]
        trashV = at("trashV", [128, D], F32, AO["trashV"])
        trashS = at("trashS", [128, D], F32, AO["trashS"])
        statc = at("statc", [128, 16], F32, AO["statc"])
        bqt = at("bqt", [H, HD], F32, AO["bq"])
        bkt = at("bkt", [H, HD], F32, AO["bk"])
        qk4 = at("qk4", [4, HD], F32, AO["qk4"])
        vv4 = at("vv4", [4, HD], F32, AO["vv4"])
        sg1 = at("sg1", [4, HD], F32, AO["sg1"])
        sg2 = at("sg2", [4, HD], F32, AO["sg2"])
        kfm = at("kfm", [128, 8, T], BF16, BO["kfm"])
        vtm = at("vtm", [128, NST, HD], BF16, BO["vtm"])
        strip = at("strip", [128, NST, 512], BF16, BO["strip"])
        qh = at("qh", [128, 8, QPC], BF16, BO["qh"])
        xq4 = at("xq4", [128, 4, 512], F32, BO["xq4"])
        ystg = at("ystg", [128, 4, 512], F32, BO["ystg"])
        praw = at("praw", [128, 2, 512], BF16, BO["praw"])
        mskt = at("mskt", [128, 512], BF16, BO["msk"])
        castb = at("castb", [128, 4, 512], BF16, 195 * KB)
        sqb = at("sqb", [128, 4, 512], BF16, 199 * KB)
        wsl1 = at("wsl1", [128, 2, 8, 512], BF16, CO["wsl1"])
        wsl3 = at("wsl3", [128, 2, 8, 512], BF16, CO["wsl3"])
        xf = at("xf", [128, 8, KB], BF16, CO["xf"])
        hf = at("hf", [128, NCF, KB], BF16, CO["hf"])
        xn2 = at("xn2", [128, 4, KB], BF16, CO["xn2"])
        ybuf = at("ybuf", [128, 2, KB], F32, CO["ybuf"])
        ybf = at("ybf", [128, 2, KB], BF16, CO["ybf"])
        ysq = at("ysq", [128, KB], BF16, CO["ysq"])
        xtmp = ysq
        silu01 = at("silu01", [128, 2, 512], BF16, CO["silu01"])
        htmp = at("htmp", [128, 2, 512], BF16, CO["htmp"])
        b1aug = at("b1aug", [2, HIDP], BF16, CO["b1aug"])
        b3aug = at("b3aug", [2, HIDP], BF16, CO["b3aug"])
        b2aug = at("b2aug", [2, D], BF16, CO["b2aug"])
        z1bf = at("z1bf", [2, KB], BF16, CO["z1bf"])
        z3bf = at("z3bf", [2, KB], BF16, CO["z3bf"])
        z2bf = at("z2bf", [2, KB], BF16, CO["z2bf"])
        rows = at("rows", [1, 4, KB], F32, CO["rows"])
        rowsb = at("rowsb", [1, 2, KB], BF16, CO["rowsb"])
        r2b = at("r2b", [128, KB], BF16, CO["r2b"])
        m2rb = at("m2rb", [128, KB], BF16, CO["m2rb"])
        w2sl = at("w2sl", [128, 2, NCF, 512], BF16, CO["w2sl"])
        ostg = at("ostg", [128, 2, KB], F32, CO["ostg"])

        P = [nc.alloc_psum_tensor(f"P{i}", [128, 512], F32) for i in range(8)]

        def sem(name):
            return ctx.enter_context(nc.semaphore(name))
        d_pro = sem("d_pro")
        d_a = [sem("d_a0"), sem("d_a1")]
        d_ax = [sem("d_ax0"), sem("d_ax1")]
        d_k = sem("d_k"); d_v = sem("d_v")
        d_k2 = sem("d_k2"); d_v2 = sem("d_v2")
        d_xq = sem("d_xq"); d_y = sem("d_y")
        d_c = [sem("d_c0"), sem("d_c1")]
        d_o = [sem("d_o0"), sem("d_o1")]
        d_wa = sem("d_wa")
        d_yb = [sem("d_yb0"), sem("d_yb1")]
        d_ws = [sem("d_ws0"), sem("d_ws1")]
        d_w2 = [sem("d_w20"), sem("d_w21")]
        v_pro = sem("v_pro"); t_pro = sem("t_pro"); v_a = sem("v_a"); s_a = sem("s_a")
        t_sc = sem("t_sc"); s_sig = sem("s_sig"); v_str = sem("v_str")
        t_pv = sem("t_pv"); v_ev = sem("v_ev"); t_hd = sem("t_hd"); v_q = sem("v_q")
        t_st = sem("t_st"); v_c0 = sem("v_c0"); v_r = sem("v_r"); s_c0 = sem("s_c0")
        t_z = sem("t_z"); v_c1 = sem("v_c1"); v_xf = sem("v_xf"); v_z = sem("v_z")
        t_ga = sem("t_ga"); s_si = sem("s_si"); v_h = sem("v_h")
        t_z2 = sem("t_z2"); v_z2 = sem("v_z2"); t_dn = sem("t_dn"); v_dn = sem("v_dn")
        t_vf = sem("t_vf")
        v_sq = sem("v_sq")

        block = ctx.enter_context(nc.Block())

        @block.sync
        def _(y):
            for dst, src in [(iotat[:], iota), (thrt[:], thr), (identt[:], ident),
                             (bqt[:], bvq), (bkt[:], bvk), (vv4[:], bvv)]:
                y.dma_start(out=dst, in_=src).then_inc(d_pro, 16)
            xln_r = xln_s.rearrange("(st p) d -> p st d", p=128)
            for i in range(32):
                if i >= 2:
                    y.wait_ge(v_a, 2 * i - 2)
                    y.wait_ge(s_a, 2 * i - 3)
                y.dma_start(out=xbuf[i % 2][:], in_=x_in[128 * i:128 * (i + 1), :]
                            ).then_inc(d_a[i % 2], 16)
                if i >= 1:
                    j = i - 1
                    y.wait_ge(v_a, 2 * j + 2)
                    y.dma_start(out=xln_s[128 * j:128 * (j + 1), :], in_=xlnb[j % 2][:]
                                ).then_inc(d_ax[j % 2], 16)
                if i == 20:
                    y.wait_ge(d_ax[0], 16 * 8)
                    y.wait_ge(d_ax[1], 16 * 8)
                    for part in range(2):
                        y.dma_start(out=vtm[:, 8 * part:8 * (part + 1), :],
                                    in_=xln_r[:, 8 * part:8 * (part + 1), 0:1024]
                                    ).then_inc(d_v2, 16)
            y.wait_ge(v_a, 64)
            y.dma_start(out=xln_s[128 * 31:128 * 32, :], in_=xlnb[1][:]
                        ).then_inc(d_ax[1], 16)
            def kv_loads(y, h):
                for j in range(8):
                    y.dma_start_transpose(
                        kfm[:, j, :],
                        xln_s[:, 1024 * h + 128 * j:1024 * h + 128 * (j + 1)]
                    ).then_inc(d_k, 16)
                for part in range(4):
                    y.wait_ge(t_vf, 4 * (h - 1) + part + 1)
                    y.dma_start(out=vtm[:, 8 * part:8 * (part + 1), :],
                                in_=xln_r[:, 8 * part:8 * (part + 1),
                                          1024 * h:1024 * (h + 1)]).then_inc(d_v, 16)

            y.wait_ge(d_ax[0], 16 * 16)
            y.wait_ge(d_ax[1], 16 * 16)
            for j2 in range(8):
                y.dma_start_transpose(
                    kfm[:, j2, 0:2048],
                    xln_s[0:2048, 128 * j2:128 * (j2 + 1)]
                ).then_inc(d_k2, 16)
            for j2 in range(8):
                y.dma_start_transpose(
                    kfm[:, j2, 2048:4096],
                    xln_s[2048:4096, 128 * j2:128 * (j2 + 1)]
                ).then_inc(d_k, 16)
            for part in range(2, 4):
                y.dma_start(out=vtm[:, 8 * part:8 * (part + 1), :],
                            in_=xln_r[:, 8 * part:8 * (part + 1), 0:1024]
                            ).then_inc(d_v, 16)
            for h in range(H):
                for G in range(2):
                    for p in range(2):
                        pi = 4 * h + 2 * G + p
                        if pi >= 1:
                            y.wait_ge(v_ev, 4 * pi)
                        r0 = 128 * (8 * h + 4 * p)
                        y.dma_start(
                            out=xq4[:],
                            in_=xqT[r0:r0 + 512, 512 * G:512 * (G + 1)]
                                .rearrange("(j p) c -> p j c", p=128)
                        ).then_inc(d_xq, 16)
                        if G == 1 and p == 1 and h + 1 < H:
                            y.wait_ge(t_sc, 48 * (h + 1))
                            kv_loads(y, h + 1)
                        y.wait_ge(v_ev, 4 * (pi + 1))
                        y.dma_start(
                            out=y_s[r0:r0 + 512, 512 * G:512 * (G + 1)]
                                .rearrange("(j p) c -> p j c", p=128),
                            in_=ystg[:]
                        ).then_inc(d_y, 16)
            y.wait_ge(d_y, 16 * 16)
            for cidx in range(32):
                jj, rr = cidx // 4, cidx % 4
                c = jj + 8 * rr
                if cidx >= 2:
                    y.wait_ge(v_c1, cidx - 1)
                y.dma_start(out=ybuf[:, cidx % 2, :], in_=y_s[128 * c:128 * (c + 1), :]
                            ).then_inc(d_c[cidx % 2], 16)
            y.wait_ge(v_c1, 32)
            for dc in range(32):
                if dc >= 2:
                    y.wait_ge(v_dn, dc - 1)
                y.dma_start(out=ybuf[:, dc % 2, :], in_=y_s[128 * dc:128 * (dc + 1), :]
                            ).then_inc(d_c[dc % 2], 16)
                y.wait_ge(v_dn, dc + 1)
                y.dma_start(out=out_d[128 * dc:128 * (dc + 1), :], in_=ostg[:, dc % 2, :]
                            ).then_inc(d_o[dc % 2], 16)
            if debug:
                y.wait_ge(v_z, 1)
                y.wait_ge(v_xf, 8)
                y.dma_start(out=xfd, in_=xf[:].rearrange("p a b -> p (a b)")
                            ).then_inc(d_pro, 16)
                y.dma_start(out=zd[0:2, 0:KB], in_=z1bf[:]).then_inc(d_pro, 16)
                y.dma_start(out=zd[0:2, KB:2*KB], in_=z3bf[:]).then_inc(d_pro, 16)
                y.wait_ge(v_z2, 1)
                y.dma_start(out=hfd, in_=hf[:].rearrange("p a b -> p (a b)")
                            ).then_inc(d_pro, 16)
                y.dma_start(out=zd[0:2, 2*KB:3*KB], in_=z2bf[:]).then_inc(d_pro, 16)
            y.wait_ge(d_o[0], 16 * 16)
            y.wait_ge(d_o[1], 16 * 16)

        @block.gpsimd
        def _(g):
            g.dma_start(out=a13sb[:], in_=a13p).then_inc(d_wa, 16)
            g.dma_start(out=a2sb[:], in_=a2p).then_inc(d_wa, 16)
            g.wait_ge(v_ev, 64)
            g.dma_start(out=b1aug[:], in_=b1T).then_inc(d_wa, 16)
            g.dma_start(out=b3aug[:], in_=b3T).then_inc(d_wa, 16)
            g.dma_start(out=b2aug[:], in_=b2T).then_inc(d_wa, 16)
            for s in range(NSLAB):
                if s >= 2:
                    g.wait_ge(t_ga, 16 * (s - 1))
                g.dma_start(out=wsl1[:, s % 2, :, :], in_=w1p[:, s, :, :]
                            ).then_inc(d_ws[s % 2], 16)
                g.dma_start(out=wsl3[:, s % 2, :, :], in_=w3p[:, s, :, :]
                            ).then_inc(d_ws[s % 2], 16)
            g.wait_ge(t_ga, 4 * NHC)
            g.wait_ge(t_z2, 2 * NHC)
            for sd in range(NW2SL):
                if sd >= 2:
                    g.wait_ge(t_dn, 4 * (sd - 1))
                g.dma_start(out=w2sl[:, sd % 2, :, :], in_=w2p[:, sd, :, :]
                            ).then_inc(d_w2[sd % 2], 16)

        @block.vector
        def _(v):
            v.memset(onescol[:], 1.0)
            v.memset(statc[:], 0.0)
            v.memset(g0w[:], 0.0)
            v.memset(g1w[:], 0.0)
            v.memset(g0w[:, 0:1], 1.0)
            v.memset(g1w[:, 32:33], 1.0)
            v.memset(onesrow[:], 1.0).then_inc(v_c0, 1)
            v.wait_ge(d_pro, 16 * 6)
            aq, ak = statc[0:4, 12:13], statc[0:4, 13:14]
            av4 = statc[0:4, 14:15]
            v.tensor_reduce(aq, bqt[:], AX, OP.add, apply_absolute_value=True)
            v.tensor_reduce(ak, bkt[:], AX, OP.add, apply_absolute_value=True)
            v.tensor_reduce(av4, vv4[:], AX, OP.add, apply_absolute_value=True)
            v.tensor_single_scalar(aq, aq, 1.0 / HD, OP.mult)
            v.tensor_single_scalar(ak, ak, 0.125 / HD, OP.mult)
            v.tensor_single_scalar(av4, av4, 1.0 / HD, OP.mult)
            for srcb, dst in [(bqt[:], sg1[:]), (bkt[:], sg2[:])]:
                v.tensor_single_scalar(dst, srcb, 0.0, OP.is_ge)
                v.tensor_scalar(dst, dst, 2.0, 1.0, OP.mult, OP.subtract)
            v.tensor_tensor(qk4[:], sg1[:], sg2[:], OP.mult)
            v.tensor_scalar(qk4[:], qk4[:], aq, ak, OP.mult, OP.mult)
            v.tensor_single_scalar(sg1[:], vv4[:], 0.0, OP.is_ge)
            v.tensor_scalar(sg1[:], sg1[:], 2.0, 1.0, OP.mult, OP.subtract)
            v.tensor_scalar(vv4[:], sg1[:], av4, None, OP.mult)
            v.drain().then_inc(v_pro, 1)
            for dc in range(8):
                v.wait_ge(t_pro, dc + 1)
                v.tensor_copy(qkcol[:, dc, :], P[6][0:128, 0:4])
                v.tensor_copy(vcolt[:, dc, :], P[7][0:128, 0:4]).then_inc(v_pro, 1)
            v.memset(rows[:], 0.0)
            c1_ = 1.0 / D
            sumc, mc = statc[:, 0:1], statc[:, 1:2]
            e2c, msqc, tc, nmc = (statc[:, 2:3], statc[:, 3:4],
                                  statc[:, 4:5], statc[:, 5:6])
            for i in range(32):
                v.wait_ge(d_a[i % 2], 16 * (i // 2 + 1))
                v.tensor_scalar(trashV[:], xbuf[i % 2][:], 0.0, None, OP.add, OP.add,
                                accum_out=sumc)
                v.drain()
                v.tensor_single_scalar(mc, sumc, c1_, OP.mult)
                v.wait_ge(s_a, 2 * i + 1)
                sqc = statc[:, 6 + i % 2:7 + i % 2]
                vpec = statc[:, 8 + i % 2:9 + i % 2]
                v.tensor_single_scalar(e2c, sqc, c1_, OP.mult)
                v.tensor_tensor(msqc, mc, mc, OP.mult)
                v.tensor_scalar(tc, msqc, -1.0, EPS, OP.mult, OP.add)
                v.tensor_tensor(vpec, tc, e2c, OP.add)
                v.drain().then_inc(v_a, 1)
                v.wait_ge(s_a, 2 * i + 2)
                v.tensor_single_scalar(nmc, mc, -1.0, OP.mult)
                if i >= 2:
                    v.wait_ge(d_ax[i % 2], 16 * ((i - 2) // 2 + 1))
                v.tensor_scalar(xlnb[i % 2][:], xbuf[i % 2][:], nmc, statc[:, 10:11],
                                OP.add, OP.mult)
                v.drain().then_inc(v_a, 1)
            def qh_build(v, h, sls):
                for j in range(8):
                    for sl in sls:
                        v.tensor_scalar(qh[:, j, 256 * sl:256 * (sl + 1)],
                                        kfm[:, j, 1024 * sl:1024 * sl + 256],
                                        qkcol[:, j, h:h + 1], None, OP.mult)
                v.drain().then_inc(v_q, 1)

            for h in range(H):
                if h == 0:
                    v.wait_ge(d_k2, 128)
                    qh_build(v, 0, (0, 1))
                else:
                    v.wait_ge(d_k, 128 * (h + 1))
                    v.wait_ge(t_sc, 48 * h)
                    qh_build(v, h, (0, 1, 2, 3))
                for G in range(2):
                    if h == 0 and G == 1:
                        v.wait_ge(d_ax[0], 16 * 16)
                        v.wait_ge(d_ax[1], 16 * 16)
                        v.wait_ge(d_k, 128)
                        qh_build(v, 0, (2, 3))
                    SG = SGRP[G]
                    base = 48 * h + 16 * G
                    gg = 2 * h + G
                    if gg >= 1:
                        v.wait_ge(t_pv, 2 * gg)
                    for st in range(SG):
                        sx = base + st
                        v.wait_ge(s_sig, sx + 1)
                        ci = 2 * (16 * G + st)
                        v.tensor_single_scalar(
                            mskt[:, 0:256], iotat[:, 0:256],
                            thrt[:, ci:ci + 1], OP.is_ge)
                        v.tensor_single_scalar(
                            mskt[:, 256:512], iotat[:, 0:256],
                            thrt[:, ci + 1:ci + 2], OP.is_ge)
                        v.tensor_tensor(strip[:, st, :], praw[:, sx % 2, :], mskt[:],
                                        OP.mult).then_inc(v_str, 1)
                    for p in range(2):
                        pi = 4 * h + 2 * G + p
                        v.wait_ge(t_pv, 2 * gg + p + 1)
                        v.wait_ge(d_xq, 16 * (pi + 1))
                        if pi >= 1:
                            v.wait_ge(d_y, 16 * pi)
                        for j in range(4):
                            dsl = 4 * p + j
                            v.scalar_tensor_tensor(
                                ystg[:, j, :], P[2 + j][:],
                                vcolt[:, dsl, h:h + 1], xq4[:, j, :],
                                OP.mult, OP.add)
                        v.drain().then_inc(v_ev, 4)
                        for j in range(4):
                            v.tensor_copy(castb[:, j, :], ystg[:, j, :])
                        for j in range(4):
                            v.tensor_tensor(sqb[:, j, :], castb[:, j, :],
                                            castb[:, j, :], OP.mult)
                        v.drain().then_inc(v_sq, 1)
            v.wait_ge(t_st, 2)
            m2r = rows[:, 0, :]
            sqr = rows[:, 1, :]
            vper = rows[:, 2, :]
            r2r = rows[:, 3, :]
            mrr = rows[:, 1, :]
            v.tensor_copy(m2r[0:1, 0:512], P[6][0:1, :])
            v.tensor_copy(m2r[0:1, 512:1024], P[6][32:33, :])
            v.tensor_copy(sqr[0:1, 0:512], P[7][0:1, :])
            v.tensor_copy(sqr[0:1, 512:1024], P[7][32:33, :])
            v.tensor_single_scalar(m2r, m2r, c1_, OP.mult)
            v.tensor_single_scalar(sqr, sqr, c1_, OP.mult)
            v.tensor_tensor(vper, m2r, m2r, OP.mult)
            v.tensor_single_scalar(vper, vper, -1.0, OP.mult)
            v.tensor_tensor(vper, vper, sqr, OP.add)
            v.tensor_single_scalar(vper, vper, EPS, OP.add)
            v.drain().then_inc(v_r, 1)
            v.wait_ge(s_c0, 1)
            v.tensor_tensor(mrr, m2r, r2r, OP.mult)
            v.tensor_copy(rowsb[:, 0, :], r2r)
            v.tensor_copy(rowsb[:, 1, :], mrr)
            v.drain().then_inc(v_r, 1)
            v.wait_ge(t_st, 6)
            v.tensor_copy(r2b[:, 0:512], P[4][:])
            v.tensor_copy(r2b[:, 512:1024], P[5][:])
            v.tensor_copy(m2rb[:, 0:512], P[6][:])
            v.tensor_copy(m2rb[:, 512:1024], P[7][:])
            v.drain().then_inc(v_r, 1)
            for cidx in range(32):
                jj, rr = cidx // 4, cidx % 4
                v.wait_ge(d_c[cidx % 2], 16 * (cidx // 2 + 1))
                if jj >= 1:
                    v.wait_ge(t_z, 4 * (jj - 1) + rr + 1)
                v.tensor_tensor(xtmp[:], ybuf[:, cidx % 2, :], r2b[:], OP.mult)
                v.tensor_tensor(xn2[:, rr, :], xtmp[:], m2rb[:], OP.subtract)
                v.drain().then_inc(v_c1, 1)
                if rr == 3:
                    v.tensor_tensor(ybf[:, 0, :], xn2[:, 0, :], xn2[:, 1, :], OP.add)
                    v.tensor_tensor(ybf[:, 1, :], xn2[:, 2, :], xn2[:, 3, :], OP.add)
                    v.tensor_tensor(xf[:, jj, :], ybf[:, 0, :], ybf[:, 1, :], OP.add)
                    v.drain().then_inc(v_xf, 1)
            v.wait_ge(t_z, 32)
            v.tensor_single_scalar(z1bf[:, 0:512], P[4][0:2, :], 0.5, OP.mult)
            v.tensor_single_scalar(z1bf[:, 512:1024], P[5][0:2, :], 0.5, OP.mult)
            v.tensor_single_scalar(z3bf[:, 0:512], P[6][0:2, :], 0.5, OP.mult)
            v.tensor_single_scalar(z3bf[:, 512:1024], P[7][0:2, :], 0.5, OP.mult)
            v.drain().then_inc(v_z, 1)
            for hc in range(NHC):
                r_, a_ = hc // NCF, hc % NCF
                tgt0 = hf[:, a_, 0:512] if r_ == 0 else htmp[:, 0, :]
                tgt1 = hf[:, a_, 512:1024] if r_ == 0 else htmp[:, 1, :]
                v.wait_ge(s_si, 2 * hc + 1)
                v.wait_ge(t_ga, 4 * hc + 2)
                if hc >= 23:
                    v.wait_ge(t_z2, 2 * hc - 1)
                v.tensor_tensor(tgt0, silu01[:, 0, :], P[1][:], OP.mult)
                if r_ > 0:
                    v.tensor_tensor(hf[:, a_, 0:512], hf[:, a_, 0:512], htmp[:, 0, :],
                                    OP.add)
                v.drain().then_inc(v_h, 1)
                v.wait_ge(s_si, 2 * hc + 2)
                v.wait_ge(t_ga, 4 * hc + 4)
                if hc >= 23:
                    v.wait_ge(t_z2, 2 * hc)
                v.tensor_tensor(tgt1, silu01[:, 1, :], P[3][:], OP.mult)
                if r_ > 0:
                    v.tensor_tensor(hf[:, a_, 512:1024], hf[:, a_, 512:1024],
                                    htmp[:, 1, :], OP.add)
                v.drain().then_inc(v_h, 1)
            v.wait_ge(t_z2, 2 * NHC)
            v.tensor_single_scalar(z2bf[:, 0:512], P[4][0:2, :], 0.5, OP.mult)
            v.tensor_single_scalar(z2bf[:, 512:1024], P[5][0:2, :], 0.5, OP.mult)
            v.drain().then_inc(v_z2, 1)
            for dc in range(32):
                pa, pb = (P[0], P[1]) if dc % 2 == 0 else (P[2], P[3])
                v.wait_ge(t_dn, dc + 1)
                v.wait_ge(d_c[dc % 2], 16 * (16 + dc // 2 + 1))
                if dc >= 2:
                    v.wait_ge(d_o[dc % 2], 16 * (dc // 2))
                v.tensor_tensor(ostg[:, dc % 2, 0:512], pa[:], ybuf[:, dc % 2, 0:512],
                                OP.add)
                v.tensor_tensor(ostg[:, dc % 2, 512:1024], pb[:],
                                ybuf[:, dc % 2, 512:1024], OP.add)
                v.drain().then_inc(v_dn, 1)

        @block.scalar
        def _(s):
            s.wait_ge(d_pro, 16 * 3)
            s.activation(statc[:, 15:16], identt[:, 1:2], AF.Square)
            s.activation(statc[:, 15:16], identt[:, 0:1], AF.Exp)
            s.activation(statc[:, 15:16], identt[:, 0:1], AF.Sigmoid)
            s.activation(statc[:, 15:16], identt[:, 0:1], AF.Silu)
            s.activation(statc[:, 15:16], identt[:, 0:1], AF.Exp)
            for i in range(32):
                s.wait_ge(d_a[i % 2], 16 * (i // 2 + 1))
                sqc = statc[:, 6 + i % 2:7 + i % 2]
                vpec = statc[:, 8 + i % 2:9 + i % 2]
                s.activation(trashS[:], xbuf[i % 2][:], AF.Square, accum_out=sqc)
                s.drain().then_inc(s_a, 1)
                s.wait_ge(v_a, 2 * i + 1)
                s.activation(statc[:, 11:12], vpec, AF.Ln)
                s.activation(statc[:, 10:11], statc[:, 11:12], AF.Exp, scale=-0.5)
                s.drain().then_inc(s_a, 1)
            for h in range(H):
                for G in range(2):
                    for st in range(SGRP[G]):
                        sx = 48 * h + 16 * G + st
                        s.wait_ge(t_sc, sx + 1)
                        if sx - 1 > 0:
                            s.wait_ge(v_str, sx - 1)
                        s.activation(praw[:, sx % 2, :], P[sx % 2][:], AF.Sigmoid)
                        s.drain().then_inc(s_sig, 1)
            s.wait_ge(v_r, 1)
            s.activation(rows[:, 3, :], rows[:, 2, :], AF.Ln)
            s.activation(rows[:, 3, :], rows[:, 3, :], AF.Exp, scale=-0.5)
            s.drain().then_inc(s_c0, 1)
            for hc in range(NHC):
                s.wait_ge(t_ga, 4 * hc + 1)
                if hc >= 1:
                    s.wait_ge(v_h, 2 * hc - 1)
                s.activation(silu01[:, 0, :], P[0][:], AF.Silu)
                s.drain().then_inc(s_si, 1)
                s.wait_ge(t_ga, 4 * hc + 3)
                s.wait_ge(v_h, 2 * hc)
                s.activation(silu01[:, 1, :], P[2][:], AF.Silu)
                s.drain().then_inc(s_si, 1)

        @block.tensor
        def _(t):
            t.wait_ge(v_pro, 1)
            for dc in range(8):
                if dc >= 1:
                    t.wait_ge(v_pro, 1 + dc)
                t.matmul(P[6][0:128, 0:4], qk4[0:4, 128 * dc:128 * (dc + 1)],
                         identt[0:4, 0:4], start=True, stop=True)
                t.matmul(P[7][0:128, 0:4], vv4[0:4, 128 * dc:128 * (dc + 1)],
                         identt[0:4, 0:4], start=True, stop=True).then_inc(t_pro, 1)
            for h in range(H):
                if h == 0:
                    t.wait_ge(v_q, 1)
                    t.wait_ge(d_k2, 128)
                else:
                    t.wait_ge(v_q, h + 2)
                    t.wait_ge(d_k, 128 * (h + 1))
                for G in range(2):
                    if h == 0 and G == 1:
                        t.wait_ge(v_q, 2)
                        t.wait_ge(d_k, 128)
                    SG = SGRP[G]
                    base = 48 * h + 16 * G
                    for st in range(SG):
                        sx = base + st
                        if sx - 1 > 0:
                            t.wait_ge(s_sig, sx - 1)
                        for j in range(8):
                            mm = t.matmul(P[sx % 2][:],
                                          kfm[:, j, 128 * st:128 * (st + 1)],
                                          qh[:, j, 512 * G:512 * (G + 1)],
                                          start=(j == 0), stop=(j == 7))
                            if j == 7:
                                mm.then_inc(t_sc, 1)
                    t.wait_ge(v_str, base + SG)
                    if h == 0:
                        t.wait_ge(d_v2 if G == 0 else d_v, 32)
                    else:
                        t.wait_ge(d_v, 32 + 64 * h)
                    for p in range(2):
                        pi = 4 * h + 2 * G + p
                        if pi >= 1:
                            pprev = pi - 1
                            t.wait_ge(v_sq, pprev + 1)
                            gw = g0w if (pprev // 2) % 2 == 0 else g1w
                            for j in range(4):
                                t.matmul(P[6][0:33, :], gw[:], castb[:, j, :],
                                         start=(pprev == 0 and j == 0),
                                         stop=False, skip_group_check=True)
                            for j in range(4):
                                t.matmul(P[7][0:33, :], gw[:], sqb[:, j, :],
                                         start=(pprev == 0 and j == 0),
                                         stop=False, skip_group_check=True)
                        if 4 * pi > 0:
                            t.wait_ge(v_ev, 4 * pi)
                        for j in range(4):
                            dsl = 4 * p + j
                            for st in range(SG):
                                mm = t.matmul(P[2 + j][:],
                                              vtm[:, st, 128 * dsl:128 * (dsl + 1)],
                                              strip[:, st, :],
                                              start=(st == 0), stop=(st == SG - 1),
                                              skip_group_check=True)
                                if st == SG - 1 and j == 3:
                                    mm.then_inc(t_pv, 1)
                                elif G == 1 and p == 1 and j == 3 and st % 8 == 7:
                                    mm.then_inc(t_vf, 1)
                        if p == 1 and G == 1:
                            t.nop().then_inc(t_vf, 1)
            t.wait_ge(v_sq, 16)
            for j in range(4):
                mm = t.matmul(P[6][0:33, :], g1w[:], castb[:, j, :],
                              start=False, stop=(j == 3), skip_group_check=True)
                if j == 3:
                    mm.then_inc(t_st, 1)
            for j in range(4):
                mm = t.matmul(P[7][0:33, :], g1w[:], sqb[:, j, :],
                              start=False, stop=(j == 3), skip_group_check=True)
                if j == 3:
                    mm.then_inc(t_st, 1)
            t.wait_ge(v_r, 2)
            t.matmul(P[4][:], onesrow[0:1, :], rowsb[0:1, 0, 0:512],
                     start=True, stop=True).then_inc(t_st, 1)
            t.matmul(P[5][:], onesrow[0:1, :], rowsb[0:1, 0, 512:1024],
                     start=True, stop=True).then_inc(t_st, 1)
            t.matmul(P[6][:], onesrow[0:1, :], rowsb[0:1, 1, 0:512],
                     start=True, stop=True).then_inc(t_st, 1)
            t.matmul(P[7][:], onesrow[0:1, :], rowsb[0:1, 1, 512:1024],
                     start=True, stop=True).then_inc(t_st, 1)
            t.wait_ge(d_wa, 32)
            t.wait_ge(v_r, 3)
            for cidx in range(32):
                jj, rr = cidx // 4, cidx % 4
                c = jj + 8 * rr
                t.wait_ge(v_c1, cidx + 1)
                st_, sp_ = (cidx == 0), (cidx == 31)
                t.matmul(P[4][0:2, :], a13sb[:, c, 0:2], xn2[:, rr, 0:512],
                         start=st_, stop=sp_, skip_group_check=True)
                t.matmul(P[5][0:2, :], a13sb[:, c, 0:2], xn2[:, rr, 512:1024],
                         start=st_, stop=sp_, skip_group_check=True)
                t.matmul(P[6][0:2, :], a13sb[:, c, 2:4], xn2[:, rr, 0:512],
                         start=st_, stop=sp_, skip_group_check=True)
                t.matmul(P[7][0:2, :], a13sb[:, c, 2:4], xn2[:, rr, 512:1024],
                         start=st_, stop=sp_, skip_group_check=True).then_inc(t_z, 1)
            t.wait_ge(v_xf, 8)
            t.wait_ge(v_z, 1)
            t.wait_ge(d_wa, 80)

            def z2rhs(hcp, half):
                r_, a_ = hcp // NCF, hcp % NCF
                if r_ == 0:
                    return hf[:, a_, 512 * half:512 * (half + 1)]
                return htmp[:, half, :]

            for hc in range(NHC):
                s_ = hc // 4
                wi = hc % 4
                if wi == 0:
                    t.wait_ge(d_ws[s_ % 2], 32 * (s_ // 2 + 1))
                if hc >= 1:
                    t.wait_ge(v_h, 2 * (hc - 1) + 1)
                    t.matmul(P[4][0:2, :], a2sb[:, hc - 1, :], z2rhs(hc - 1, 0),
                             start=(hc - 1 == 0), stop=False, skip_group_check=True
                             ).then_inc(t_z2, 1)
                    t.wait_ge(s_si, 2 * (hc - 1) + 1)
                for cch in range(8):
                    t.matmul(P[0][:], wsl1[:, s_ % 2, cch, 128 * wi:128 * (wi + 1)],
                             xf[:, cch, 0:512], start=(cch == 0), stop=False,
                             skip_group_check=True)
                t.matmul(P[0][:], b1aug[0:2, 128 * hc:128 * (hc + 1)], z1bf[0:2, 0:512],
                         start=False, stop=True, skip_group_check=True
                         ).then_inc(t_ga, 1)
                for cch in range(8):
                    t.matmul(P[1][:], wsl3[:, s_ % 2, cch, 128 * wi:128 * (wi + 1)],
                             xf[:, cch, 0:512], start=(cch == 0), stop=False,
                             skip_group_check=True)
                t.matmul(P[1][:], b3aug[0:2, 128 * hc:128 * (hc + 1)], z3bf[0:2, 0:512],
                         start=False, stop=True, skip_group_check=True
                         ).then_inc(t_ga, 1)
                if hc >= 1:
                    t.wait_ge(v_h, 2 * (hc - 1) + 2)
                    t.matmul(P[5][0:2, :], a2sb[:, hc - 1, :], z2rhs(hc - 1, 1),
                             start=(hc - 1 == 0), stop=False, skip_group_check=True
                             ).then_inc(t_z2, 1)
                    t.wait_ge(s_si, 2 * (hc - 1) + 2)
                    t.wait_ge(v_h, 2 * hc)
                for cch in range(8):
                    t.matmul(P[2][:], wsl1[:, s_ % 2, cch, 128 * wi:128 * (wi + 1)],
                             xf[:, cch, 512:1024], start=(cch == 0), stop=False,
                             skip_group_check=True)
                t.matmul(P[2][:], b1aug[0:2, 128 * hc:128 * (hc + 1)],
                         z1bf[0:2, 512:1024], start=False, stop=True,
                         skip_group_check=True).then_inc(t_ga, 1)
                for cch in range(8):
                    t.matmul(P[3][:], wsl3[:, s_ % 2, cch, 128 * wi:128 * (wi + 1)],
                             xf[:, cch, 512:1024], start=(cch == 0), stop=False,
                             skip_group_check=True)
                t.matmul(P[3][:], b3aug[0:2, 128 * hc:128 * (hc + 1)],
                         z3bf[0:2, 512:1024], start=False, stop=True,
                         skip_group_check=True).then_inc(t_ga, 1)
            t.wait_ge(v_h, 2 * NHC - 1)
            t.matmul(P[4][0:2, :], a2sb[:, NHC - 1, :], z2rhs(NHC - 1, 0),
                     start=False, stop=True, skip_group_check=True).then_inc(t_z2, 1)
            t.wait_ge(v_h, 2 * NHC)
            t.matmul(P[5][0:2, :], a2sb[:, NHC - 1, :], z2rhs(NHC - 1, 1),
                     start=False, stop=True, skip_group_check=True).then_inc(t_z2, 1)
            t.wait_ge(v_z2, 1)
            t.wait_ge(s_si, 2 * NHC)
            for dc in range(32):
                sd, wi2 = dc // 4, dc % 4
                pa, pb = (P[0], P[1]) if dc % 2 == 0 else (P[2], P[3])
                if wi2 == 0:
                    t.wait_ge(d_w2[sd % 2], 16 * (sd // 2 + 1))
                if dc >= 2:
                    t.wait_ge(v_dn, dc - 1)
                for cf in range(NCF):
                    t.matmul(pa[:], w2sl[:, sd % 2, cf, 128 * wi2:128 * (wi2 + 1)],
                             hf[:, cf, 0:512], start=(cf == 0), stop=False,
                             skip_group_check=True)
                t.matmul(pa[:], b2aug[0:2, 128 * dc:128 * (dc + 1)], z2bf[0:2, 0:512],
                         start=False, stop=True, skip_group_check=True)
                for cf in range(NCF):
                    t.matmul(pb[:], w2sl[:, sd % 2, cf, 128 * wi2:128 * (wi2 + 1)],
                             hf[:, cf, 512:1024], start=(cf == 0), stop=False,
                             skip_group_check=True)
                t.matmul(pb[:], b2aug[0:2, 128 * dc:128 * (dc + 1)], z2bf[0:2, 512:1024],
                         start=False, stop=True, skip_group_check=True
                         ).then_inc(t_dn, 1)

    return nc


_NC_CACHE = {}


def _get_nc():
    if "nc" not in _NC_CACHE:
        _NC_CACHE["nc"] = build_nc()
    return _NC_CACHE["nc"]


def _pad_hidden(a, axis):
    sh = list(a.shape)
    assert sh[axis] == 11008
    a2 = np.moveaxis(np.asarray(a, np.float32), axis, 0)
    rest = a2.shape[1:]
    src = a2.reshape(4, 2752, *rest)
    dst = np.zeros((4, 2816, *rest), np.float32)
    dst[:, :2752] = src
    out = dst.reshape(HIDP, *rest)
    return np.moveaxis(out, 0, axis)


def _prep_consts(bv_q, bv_k, bv_v, w1q, a1, b1, w3q, a3, b3, w2q, a2, b2):
    f = np.float32
    c = {}
    c["iota"] = np.ascontiguousarray(
        np.broadcast_to(np.arange(512, dtype=f), (128, 512)))
    c["ident"] = np.eye(128, dtype=f)
    c["bvq"] = np.ascontiguousarray(np.asarray(bv_q, f))
    c["bvk"] = np.ascontiguousarray(np.asarray(bv_k, f))
    c["bvv"] = np.ascontiguousarray(np.asarray(bv_v, f))

    def prep_w13(wq):
        wT = np.asarray(wq, f).T
        wTp = _pad_hidden(wT, 1)
        a = wTp.reshape(8, 128, HIDP).transpose(1, 0, 2)
        a = a.reshape(128, 8, NSLAB, 512).transpose(0, 2, 1, 3)
        return np.ascontiguousarray(a)

    c["w1p"] = prep_w13(w1q)
    c["w3p"] = prep_w13(w3q)
    w2T = np.asarray(w2q, f).T
    w2Tp = np.zeros((CFP, D), f)
    w2Tp[:2752] = w2T
    a = w2Tp.reshape(NCF, 128, D).transpose(1, 0, 2)
    a = a.reshape(128, NCF, NW2SL, 512).transpose(0, 2, 1, 3)
    c["w2p"] = np.ascontiguousarray(a)
    a13 = np.concatenate([np.asarray(a1, f), np.asarray(a3, f)], 0)
    c["a13p"] = np.ascontiguousarray(a13.T.reshape(32, 128, 4).transpose(1, 0, 2))
    a2p_ = _pad_hidden(np.asarray(a2, f), 1)
    c["a2p"] = np.ascontiguousarray(a2p_.T.reshape(NHC, 128, 2).transpose(1, 0, 2))
    c["b1T"] = np.ascontiguousarray(_pad_hidden(np.asarray(b1, f), 0).T)
    c["b3T"] = np.ascontiguousarray(_pad_hidden(np.asarray(b3, f), 0).T)
    c["b2T"] = np.ascontiguousarray(np.asarray(b2, f).T)
    return c


def _core_layout(k):
    chunks = [4 * i + k for i in range(4)]
    crow = [np.arange(256 * j, 256 * j + 256) for j in chunks]
    inchunk = np.zeros(T, bool)
    for r in crow:
        inchunk[r] = True
    nonchunk = np.nonzero(~inchunk)[0]
    parts = []
    for i in range(4):
        parts.append(crow[i])
        parts.append(nonchunk[768 * i:768 * (i + 1)])
    perm = np.concatenate(parts)
    qrows = np.concatenate(crow)
    return perm, qrows, chunks


def _core_inputs(consts, x, c):
    b, k = c // 4, c % 4
    perm, qrows, chunks = _core_layout(k)
    m = dict(consts)
    m["x"] = np.ascontiguousarray(x[b][perm])
    m["xqT"] = np.ascontiguousarray(x[b][qrows].T)
    thr_v = np.zeros((128, 96), np.float32)
    p = np.arange(128)
    for G in range(2):
        for st in range(SGRP[G]):
            gkey = perm[128 * st + p].astype(np.float32)
            for half in range(2):
                slot = 2 * G + half
                thr_v[:, 2 * (16 * G + st) + half] = gkey - 256 * chunks[slot]
    m["thr"] = np.ascontiguousarray(thr_v)
    return m


def kernel(L0_attn_x, L0_attn_bv_q, L0_attn_bv_k, L0_attn_bv_v, L0_ffn_w1q, L0_ffn_a1,
           L0_ffn_b1, L0_ffn_w3q, L0_ffn_a3, L0_ffn_b3, L0_ffn_w2q, L0_ffn_a2,
           L0_ffn_b2):
    x = np.asarray(L0_attn_x, np.float32)
    B = x.shape[0]
    consts = _prep_consts(L0_attn_bv_q, L0_attn_bv_k, L0_attn_bv_v, L0_ffn_w1q,
                          L0_ffn_a1, L0_ffn_b1, L0_ffn_w3q, L0_ffn_a3, L0_ffn_b3,
                          L0_ffn_w2q, L0_ffn_a2, L0_ffn_b2)
    in_maps = [_core_inputs(consts, x, c) for c in range(8)]
    nc = _get_nc()
    res = run_bass_kernel_spmd(nc, in_maps, core_ids=list(range(8)))
    out = np.empty((B, T, D), np.float32)
    for c in range(8):
        b, k = c // 4, c % 4
        _, qrows, _ = _core_layout(k)
        out[b, qrows, :] = res.results[c]["out"].T
    return out


# version 15
# speedup vs baseline: 1.0240x; 1.0240x over previous
import numpy as np
from contextlib import ExitStack

import concourse.bass as bass
import concourse.mybir as mybir
from concourse.bass_utils import run_bass_kernel_spmd

F32 = mybir.dt.float32
BF16 = mybir.dt.bfloat16
U8 = mybir.dt.uint8
AF = mybir.ActivationFunctionType
OP = mybir.AluOpType
AX = mybir.AxisListType.X

T = 4096
D = 4096
H = 4
HD = 1024
QPC = 1024
NQT = 2
NST = 32
HIDP = 11264
NHC = 88
CFP = 2816
NCF = 22
NSLAB = 22
NW2SL = 8
SGRP = (16, 32)
EPS = 1e-5
KB = 1024

AO = dict(xbuf0=96*KB, xbuf1=112*KB, xlnb0=144*KB, xlnb1=152*KB, trashV=128*KB,
          trashS=160*KB, statc=202*KB, bq=177*KB, bk=181*KB, qk4=185*KB,
          vv4=189*KB, sg1=193*KB, sg2=197*KB)
BO = dict(kfm=0, vtm=64*KB, strip=128*KB, qh=160*KB, xq4=176*KB,
          ystg=184*KB, praw=192*KB, msk=194*KB)
CO = dict(wsl1=0, wsl3=16*KB, xf=32*KB, hf=48*KB, xn2=92*KB, ybuf=100*KB,
          ybf=108*KB, ysq=112*KB, silu01=114*KB, htmp=116*KB,
          b1aug=118*KB, b3aug=140*KB, b2aug=162*KB,
          z1bf=170*KB, z3bf=172*KB, z2bf=174*KB,
          rows=176*KB, rowsb=192*KB, r2b=196*KB, m2rb=198*KB,
          w2sl=0, ostg=92*KB)


def build_nc(debug=False):
    nc = bass.Bass("TRN2", target_bir_lowering=False)

    x_in = nc.dram_tensor("x", [T, D], F32, kind="ExternalInput").ap()
    xqT = nc.dram_tensor("xqT", [D, QPC], F32, kind="ExternalInput").ap()
    thr = nc.dram_tensor("thr", [128, 96], F32, kind="ExternalInput").ap()
    iota = nc.dram_tensor("iota", [128, 512], F32, kind="ExternalInput").ap()
    ident = nc.dram_tensor("ident", [128, 128], F32, kind="ExternalInput").ap()
    bvq = nc.dram_tensor("bvq", [H, HD], F32, kind="ExternalInput").ap()
    bvk = nc.dram_tensor("bvk", [H, HD], F32, kind="ExternalInput").ap()
    bvv = nc.dram_tensor("bvv", [H, HD], F32, kind="ExternalInput").ap()
    w1p = nc.dram_tensor("w1p", [128, NSLAB, 8, 512], F32, kind="ExternalInput").ap()
    w3p = nc.dram_tensor("w3p", [128, NSLAB, 8, 512], F32, kind="ExternalInput").ap()
    w2p = nc.dram_tensor("w2p", [128, NW2SL, NCF, 512], F32, kind="ExternalInput").ap()
    a13p = nc.dram_tensor("a13p", [128, 32, 4], F32, kind="ExternalInput").ap()
    a2p = nc.dram_tensor("a2p", [128, NHC, 2], F32, kind="ExternalInput").ap()
    b1T = nc.dram_tensor("b1T", [2, HIDP], F32, kind="ExternalInput").ap()
    b3T = nc.dram_tensor("b3T", [2, HIDP], F32, kind="ExternalInput").ap()
    b2T = nc.dram_tensor("b2T", [2, D], F32, kind="ExternalInput").ap()

    out_d = nc.dram_tensor("out", [D, QPC], F32, kind="ExternalOutput").ap()
    xln_s = nc.dram_tensor("xln_s", [T, D], BF16,
                           kind="ExternalOutput" if debug else "Internal").ap()
    y_s = nc.dram_tensor("y_s", [D, QPC], F32,
                         kind="ExternalOutput" if debug else "Internal").ap()
    xfd = (nc.dram_tensor("xfd", [128, 8 * KB], BF16, kind="ExternalOutput").ap()
           if debug else None)
    hfd = (nc.dram_tensor("hfd", [128, NCF * KB], BF16, kind="ExternalOutput").ap()
           if debug else None)
    zd = (nc.dram_tensor("zd", [2, 3 * KB], BF16, kind="ExternalOutput").ap()
          if debug else None)

    with ExitStack() as ctx:
        def sb(name, shape, dtype):
            return nc.alloc_sbuf_tensor(name, shape, dtype)
        iotat = sb("iotat", [128, 512], F32)
        thrt = sb("thrt", [128, 96], F32)
        identt = sb("identt", [128, 128], F32)
        qkcol = sb("qkcol", [128, 8, 4], F32)
        vcolt = sb("vcolt", [128, 8, 4], F32)
        a13sb = sb("a13sb", [128, 32, 4], BF16)
        a2sb = sb("a2sb", [128, NHC, 2], BF16)
        onescol = sb("onescol", [128, 1], BF16)
        onesrow = sb("onesrow", [1, 128], BF16)
        g0w = sb("g0w", [128, 33], BF16)
        g1w = sb("g1w", [128, 33], BF16)

        arena = nc.alloc_sbuf_tensor("arena", [128, 203 * KB], U8)
        abase = None
        for alloc in nc.m.functions[0].allocations:
            try:
                ml = alloc.memorylocations[0]
                if ml.name == "arena":
                    abase = ml.addr
            except Exception:
                pass
        assert abase is not None

        def at(name, shape, dtype, off):
            return nc.alloc_sbuf_tensor_at(name, shape, dtype, offset=abase + off)

        xbuf = [at("xbuf0", [128, D], F32, AO["xbuf0"]),
                at("xbuf1", [128, D], F32, AO["xbuf1"])]
        xlnb = [at("xlnb0", [128, D], BF16, AO["xlnb0"]),
                at("xlnb1", [128, D], BF16, AO["xlnb1"])]
        trashV = at("trashV", [128, D], F32, AO["trashV"])
        trashS = at("trashS", [128, D], F32, AO["trashS"])
        statc = at("statc", [128, 16], F32, AO["statc"])
        bqt = at("bqt", [H, HD], F32, AO["bq"])
        bkt = at("bkt", [H, HD], F32, AO["bk"])
        qk4 = at("qk4", [4, HD], F32, AO["qk4"])
        vv4 = at("vv4", [4, HD], F32, AO["vv4"])
        sg1 = at("sg1", [4, HD], F32, AO["sg1"])
        sg2 = at("sg2", [4, HD], F32, AO["sg2"])
        kfm = at("kfm", [128, 8, T], BF16, BO["kfm"])
        vtm = at("vtm", [128, NST, HD], BF16, BO["vtm"])
        strip = at("strip", [128, NST, 512], BF16, BO["strip"])
        qh = at("qh", [128, 8, QPC], BF16, BO["qh"])
        xq4 = at("xq4", [128, 4, 512], F32, BO["xq4"])
        ystg = at("ystg", [128, 4, 512], F32, BO["ystg"])
        praw = at("praw", [128, 2, 512], BF16, BO["praw"])
        mskt = at("mskt", [128, 512], BF16, BO["msk"])
        castb = at("castb", [128, 4, 512], BF16, 195 * KB)
        sqb = at("sqb", [128, 4, 512], BF16, 199 * KB)
        wsl1 = at("wsl1", [128, 2, 8, 512], BF16, CO["wsl1"])
        wsl3 = at("wsl3", [128, 2, 8, 512], BF16, CO["wsl3"])
        xf = at("xf", [128, 8, KB], BF16, CO["xf"])
        hf = at("hf", [128, NCF, KB], BF16, CO["hf"])
        xn2 = at("xn2", [128, 4, KB], BF16, CO["xn2"])
        ybuf = at("ybuf", [128, 2, KB], F32, CO["ybuf"])
        ybf = at("ybf", [128, 2, KB], BF16, CO["ybf"])
        ysq = at("ysq", [128, KB], BF16, CO["ysq"])
        xtmp = ysq
        silu01 = at("silu01", [128, 2, 512], BF16, CO["silu01"])
        htmp = at("htmp", [128, 2, 512], BF16, CO["htmp"])
        b1aug = at("b1aug", [2, HIDP], BF16, CO["b1aug"])
        b3aug = at("b3aug", [2, HIDP], BF16, CO["b3aug"])
        b2aug = at("b2aug", [2, D], BF16, CO["b2aug"])
        z1bf = at("z1bf", [2, KB], BF16, CO["z1bf"])
        z3bf = at("z3bf", [2, KB], BF16, CO["z3bf"])
        z2bf = at("z2bf", [2, KB], BF16, CO["z2bf"])
        rows = at("rows", [1, 4, KB], F32, CO["rows"])
        rowsb = at("rowsb", [1, 2, KB], BF16, CO["rowsb"])
        r2b = at("r2b", [128, KB], BF16, CO["r2b"])
        m2rb = at("m2rb", [128, KB], BF16, CO["m2rb"])
        w2sl = at("w2sl", [128, 2, NCF, 512], BF16, CO["w2sl"])
        ostg = at("ostg", [128, 2, KB], F32, CO["ostg"])

        P = [nc.alloc_psum_tensor(f"P{i}", [128, 512], F32) for i in range(8)]

        def sem(name):
            return ctx.enter_context(nc.semaphore(name))
        d_pro = sem("d_pro")
        d_a = [sem("d_a0"), sem("d_a1")]
        d_ax = [sem("d_ax0"), sem("d_ax1")]
        d_k = sem("d_k"); d_v = sem("d_v")
        d_k2 = sem("d_k2"); d_v2 = sem("d_v2")
        d_xq = sem("d_xq"); d_y = sem("d_y")
        d_c = [sem("d_c0"), sem("d_c1")]
        d_o = [sem("d_o0"), sem("d_o1")]
        d_wa = sem("d_wa")
        d_yb = [sem("d_yb0"), sem("d_yb1")]
        d_ws = [sem("d_ws0"), sem("d_ws1")]
        d_w2 = [sem("d_w20"), sem("d_w21")]
        v_pro = sem("v_pro"); t_pro = sem("t_pro"); v_a = sem("v_a"); s_a = sem("s_a")
        t_sc = sem("t_sc"); s_sig = sem("s_sig"); v_str = sem("v_str")
        t_pv = sem("t_pv"); v_ev = sem("v_ev"); t_hd = sem("t_hd"); v_q = sem("v_q")
        t_st = sem("t_st"); v_c0 = sem("v_c0"); v_r = sem("v_r"); s_c0 = sem("s_c0")
        t_z = sem("t_z"); v_c1 = sem("v_c1"); v_xf = sem("v_xf"); v_z = sem("v_z")
        t_ga = sem("t_ga"); s_si = sem("s_si"); v_h = sem("v_h")
        t_z2 = sem("t_z2"); v_z2 = sem("v_z2"); t_dn = sem("t_dn"); v_dn = sem("v_dn")
        t_vf = sem("t_vf")
        v_sq = sem("v_sq")

        block = ctx.enter_context(nc.Block())

        @block.sync
        def _(y):
            for dst, src in [(iotat[:], iota), (thrt[:], thr), (identt[:], ident),
                             (bqt[:], bvq), (bkt[:], bvk), (vv4[:], bvv)]:
                y.dma_start(out=dst, in_=src).then_inc(d_pro, 16)
            xln_r = xln_s.rearrange("(st p) d -> p st d", p=128)
            for i in range(32):
                if i >= 2:
                    y.wait_ge(v_a, 2 * i - 2)
                    y.wait_ge(s_a, 2 * i - 3)
                y.dma_start(out=xbuf[i % 2][:], in_=x_in[128 * i:128 * (i + 1), :]
                            ).then_inc(d_a[i % 2], 16)
                if i >= 1:
                    j = i - 1
                    y.wait_ge(v_a, 2 * j + 2)
                    y.dma_start(out=xln_s[128 * j:128 * (j + 1), :], in_=xlnb[j % 2][:]
                                ).then_inc(d_ax[j % 2], 16)
                if i == 20:
                    y.wait_ge(d_ax[0], 16 * 8)
                    y.wait_ge(d_ax[1], 16 * 8)
                    for part in range(2):
                        y.dma_start(out=vtm[:, 8 * part:8 * (part + 1), :],
                                    in_=xln_r[:, 8 * part:8 * (part + 1), 0:1024]
                                    ).then_inc(d_v2, 16)
            y.wait_ge(v_a, 64)
            y.dma_start(out=xln_s[128 * 31:128 * 32, :], in_=xlnb[1][:]
                        ).then_inc(d_ax[1], 16)
            def kv_loads(y, h):
                for j in range(8):
                    y.dma_start_transpose(
                        kfm[:, j, :],
                        xln_s[:, 1024 * h + 128 * j:1024 * h + 128 * (j + 1)]
                    ).then_inc(d_k, 16)
                for part in range(4):
                    y.wait_ge(t_vf, 4 * (h - 1) + part + 1)
                    y.dma_start(out=vtm[:, 8 * part:8 * (part + 1), :],
                                in_=xln_r[:, 8 * part:8 * (part + 1),
                                          1024 * h:1024 * (h + 1)]).then_inc(d_v, 16)

            y.wait_ge(d_ax[0], 16 * 16)
            y.wait_ge(d_ax[1], 16 * 16)
            for j2 in range(8):
                y.dma_start_transpose(
                    kfm[:, j2, 0:2048],
                    xln_s[0:2048, 128 * j2:128 * (j2 + 1)]
                ).then_inc(d_k2, 16)
            for j2 in range(8):
                y.dma_start_transpose(
                    kfm[:, j2, 2048:4096],
                    xln_s[2048:4096, 128 * j2:128 * (j2 + 1)]
                ).then_inc(d_k, 16)
            for part in range(2, 4):
                y.dma_start(out=vtm[:, 8 * part:8 * (part + 1), :],
                            in_=xln_r[:, 8 * part:8 * (part + 1), 0:1024]
                            ).then_inc(d_v, 16)
            for h in range(H):
                for G in range(2):
                    for p in range(2):
                        pi = 4 * h + 2 * G + p
                        if pi >= 1:
                            y.wait_ge(v_ev, 4 * pi)
                        r0 = 128 * (8 * h + 4 * p)
                        y.dma_start(
                            out=xq4[:],
                            in_=xqT[r0:r0 + 512, 512 * G:512 * (G + 1)]
                                .rearrange("(j p) c -> p j c", p=128)
                        ).then_inc(d_xq, 16)
                        if G == 1 and p == 1 and h + 1 < H:
                            y.wait_ge(t_sc, 48 * (h + 1))
                            kv_loads(y, h + 1)
                        y.wait_ge(v_ev, 4 * (pi + 1))
                        y.dma_start(
                            out=y_s[r0:r0 + 512, 512 * G:512 * (G + 1)]
                                .rearrange("(j p) c -> p j c", p=128),
                            in_=ystg[:]
                        ).then_inc(d_y, 16)
            y.wait_ge(d_y, 16 * 16)
            for cidx in range(32):
                jj, rr = cidx // 4, cidx % 4
                c = jj + 8 * rr
                if cidx >= 2:
                    y.wait_ge(v_c1, cidx - 1)
                y.dma_start(out=ybuf[:, cidx % 2, :], in_=y_s[128 * c:128 * (c + 1), :]
                            ).then_inc(d_c[cidx % 2], 16)
            y.wait_ge(v_c1, 32)
            for dc in range(32):
                if dc >= 2:
                    y.wait_ge(v_dn, dc - 1)
                y.dma_start(out=ybuf[:, dc % 2, :], in_=y_s[128 * dc:128 * (dc + 1), :]
                            ).then_inc(d_c[dc % 2], 16)
                y.wait_ge(v_dn, dc + 1)
                y.dma_start(out=out_d[128 * dc:128 * (dc + 1), :], in_=ostg[:, dc % 2, :]
                            ).then_inc(d_o[dc % 2], 16)
            if debug:
                y.wait_ge(v_z, 1)
                y.wait_ge(v_xf, 8)
                y.dma_start(out=xfd, in_=xf[:].rearrange("p a b -> p (a b)")
                            ).then_inc(d_pro, 16)
                y.dma_start(out=zd[0:2, 0:KB], in_=z1bf[:]).then_inc(d_pro, 16)
                y.dma_start(out=zd[0:2, KB:2*KB], in_=z3bf[:]).then_inc(d_pro, 16)
                y.wait_ge(v_z2, 1)
                y.dma_start(out=hfd, in_=hf[:].rearrange("p a b -> p (a b)")
                            ).then_inc(d_pro, 16)
                y.dma_start(out=zd[0:2, 2*KB:3*KB], in_=z2bf[:]).then_inc(d_pro, 16)
            y.wait_ge(d_o[0], 16 * 16)
            y.wait_ge(d_o[1], 16 * 16)

        @block.gpsimd
        def _(g):
            g.dma_start(out=a13sb[:], in_=a13p).then_inc(d_wa, 16)
            g.dma_start(out=a2sb[:], in_=a2p).then_inc(d_wa, 16)
            g.wait_ge(v_ev, 64)
            g.dma_start(out=b1aug[:], in_=b1T).then_inc(d_wa, 16)
            g.dma_start(out=b3aug[:], in_=b3T).then_inc(d_wa, 16)
            g.dma_start(out=b2aug[:], in_=b2T).then_inc(d_wa, 16)
            for s in range(NSLAB):
                if s >= 2:
                    g.wait_ge(t_ga, 16 * (s - 1))
                g.dma_start(out=wsl1[:, s % 2, :, :], in_=w1p[:, s, :, :]
                            ).then_inc(d_ws[s % 2], 16)
                g.dma_start(out=wsl3[:, s % 2, :, :], in_=w3p[:, s, :, :]
                            ).then_inc(d_ws[s % 2], 16)
            g.wait_ge(t_ga, 4 * NHC)
            g.wait_ge(t_z2, 2 * NHC)
            for sd in range(NW2SL):
                if sd >= 2:
                    g.wait_ge(t_dn, 4 * (sd - 1))
                g.dma_start(out=w2sl[:, sd % 2, :, :], in_=w2p[:, sd, :, :]
                            ).then_inc(d_w2[sd % 2], 16)

        @block.vector
        def _(v):
            v.memset(onescol[:], 1.0)
            v.memset(statc[:], 0.0)
            v.memset(g0w[:], 0.0)
            v.memset(g1w[:], 0.0)
            v.tensor_single_scalar(g0w[:, 0:1], g0w[:, 0:1], 1.0, OP.add)
            v.tensor_single_scalar(g1w[:, 32:33], g1w[:, 32:33], 1.0, OP.add)
            v.memset(onesrow[:], 1.0).then_inc(v_c0, 1)
            v.wait_ge(d_pro, 16 * 6)
            aq, ak = statc[0:4, 12:13], statc[0:4, 13:14]
            av4 = statc[0:4, 14:15]
            v.tensor_reduce(aq, bqt[:], AX, OP.add, apply_absolute_value=True)
            v.tensor_reduce(ak, bkt[:], AX, OP.add, apply_absolute_value=True)
            v.tensor_reduce(av4, vv4[:], AX, OP.add, apply_absolute_value=True)
            v.tensor_single_scalar(aq, aq, 1.0 / HD, OP.mult)
            v.tensor_single_scalar(ak, ak, 0.125 / HD, OP.mult)
            v.tensor_single_scalar(av4, av4, 1.0 / HD, OP.mult)
            for srcb, dst in [(bqt[:], sg1[:]), (bkt[:], sg2[:])]:
                v.tensor_single_scalar(dst, srcb, 0.0, OP.is_ge)
                v.tensor_scalar(dst, dst, 2.0, 1.0, OP.mult, OP.subtract)
            v.tensor_tensor(qk4[:], sg1[:], sg2[:], OP.mult)
            v.tensor_scalar(qk4[:], qk4[:], aq, ak, OP.mult, OP.mult)
            v.tensor_single_scalar(sg1[:], vv4[:], 0.0, OP.is_ge)
            v.tensor_scalar(sg1[:], sg1[:], 2.0, 1.0, OP.mult, OP.subtract)
            v.tensor_scalar(vv4[:], sg1[:], av4, None, OP.mult)
            v.drain().then_inc(v_pro, 1)
            for dc in range(8):
                v.wait_ge(t_pro, dc + 1)
                v.tensor_copy(qkcol[:, dc, :], P[6][0:128, 0:4])
                v.tensor_copy(vcolt[:, dc, :], P[7][0:128, 0:4]).then_inc(v_pro, 1)
            v.memset(rows[:], 0.0)
            c1_ = 1.0 / D
            sumc, mc = statc[:, 0:1], statc[:, 1:2]
            e2c, msqc, tc, nmc = (statc[:, 2:3], statc[:, 3:4],
                                  statc[:, 4:5], statc[:, 5:6])
            for i in range(32):
                v.wait_ge(d_a[i % 2], 16 * (i // 2 + 1))
                v.tensor_scalar(trashV[:], xbuf[i % 2][:], 0.0, None, OP.add, OP.add,
                                accum_out=sumc)
                v.drain()
                v.tensor_single_scalar(mc, sumc, c1_, OP.mult)
                v.wait_ge(s_a, 2 * i + 1)
                sqc = statc[:, 6 + i % 2:7 + i % 2]
                vpec = statc[:, 8 + i % 2:9 + i % 2]
                v.tensor_single_scalar(e2c, sqc, c1_, OP.mult)
                v.tensor_tensor(msqc, mc, mc, OP.mult)
                v.tensor_scalar(tc, msqc, -1.0, EPS, OP.mult, OP.add)
                v.tensor_tensor(vpec, tc, e2c, OP.add)
                v.drain().then_inc(v_a, 1)
                v.wait_ge(s_a, 2 * i + 2)
                v.tensor_single_scalar(nmc, mc, -1.0, OP.mult)
                if i >= 2:
                    v.wait_ge(d_ax[i % 2], 16 * ((i - 2) // 2 + 1))
                v.tensor_scalar(xlnb[i % 2][:], xbuf[i % 2][:], nmc, statc[:, 10:11],
                                OP.add, OP.mult)
                v.drain().then_inc(v_a, 1)
            def qh_build(v, h, sls):
                for j in range(8):
                    for sl in sls:
                        v.tensor_scalar(qh[:, j, 256 * sl:256 * (sl + 1)],
                                        kfm[:, j, 1024 * sl:1024 * sl + 256],
                                        qkcol[:, j, h:h + 1], None, OP.mult)
                v.drain().then_inc(v_q, 1)

            for h in range(H):
                if h == 0:
                    v.wait_ge(d_k2, 128)
                    qh_build(v, 0, (0, 1))
                else:
                    v.wait_ge(d_k, 128 * (h + 1))
                    v.wait_ge(t_sc, 48 * h)
                    qh_build(v, h, (0, 1, 2, 3))
                for G in range(2):
                    if h == 0 and G == 1:
                        v.wait_ge(d_ax[0], 16 * 16)
                        v.wait_ge(d_ax[1], 16 * 16)
                        v.wait_ge(d_k, 128)
                        qh_build(v, 0, (2, 3))
                    SG = SGRP[G]
                    base = 48 * h + 16 * G
                    gg = 2 * h + G
                    if gg >= 1:
                        v.wait_ge(t_pv, 2 * gg)
                    for st in range(SG):
                        sx = base + st
                        v.wait_ge(s_sig, sx + 1)
                        ci = 2 * (16 * G + st)
                        v.tensor_single_scalar(
                            mskt[:, 0:256], iotat[:, 0:256],
                            thrt[:, ci:ci + 1], OP.is_ge)
                        v.tensor_single_scalar(
                            mskt[:, 256:512], iotat[:, 0:256],
                            thrt[:, ci + 1:ci + 2], OP.is_ge)
                        v.tensor_tensor(strip[:, st, :], praw[:, sx % 2, :], mskt[:],
                                        OP.mult).then_inc(v_str, 1)
                    for p in range(2):
                        pi = 4 * h + 2 * G + p
                        v.wait_ge(t_pv, 2 * gg + p + 1)
                        v.wait_ge(d_xq, 16 * (pi + 1))
                        if pi >= 1:
                            v.wait_ge(d_y, 16 * pi)
                        for j in range(4):
                            dsl = 4 * p + j
                            v.scalar_tensor_tensor(
                                ystg[:, j, :], P[2 + j][:],
                                vcolt[:, dsl, h:h + 1], xq4[:, j, :],
                                OP.mult, OP.add)
                        v.drain().then_inc(v_ev, 4)
                        for j in range(4):
                            v.tensor_copy(castb[:, j, :], ystg[:, j, :])
                        for j in range(4):
                            v.tensor_tensor(sqb[:, j, :], castb[:, j, :],
                                            castb[:, j, :], OP.mult)
                        v.drain().then_inc(v_sq, 1)
            v.wait_ge(t_st, 2)
            m2r = rows[:, 0, :]
            sqr = rows[:, 1, :]
            vper = rows[:, 2, :]
            r2r = rows[:, 3, :]
            mrr = rows[:, 1, :]
            v.tensor_copy(m2r[0:1, 0:512], P[6][0:1, :])
            v.tensor_copy(m2r[0:1, 512:1024], P[6][32:33, :])
            v.tensor_copy(sqr[0:1, 0:512], P[7][0:1, :])
            v.tensor_copy(sqr[0:1, 512:1024], P[7][32:33, :])
            v.tensor_single_scalar(m2r, m2r, c1_, OP.mult)
            v.tensor_single_scalar(sqr, sqr, c1_, OP.mult)
            v.tensor_tensor(vper, m2r, m2r, OP.mult)
            v.tensor_single_scalar(vper, vper, -1.0, OP.mult)
            v.tensor_tensor(vper, vper, sqr, OP.add)
            v.tensor_single_scalar(vper, vper, EPS, OP.add)
            v.drain().then_inc(v_r, 1)
            v.wait_ge(s_c0, 1)
            v.tensor_tensor(mrr, m2r, r2r, OP.mult)
            v.tensor_copy(rowsb[:, 0, :], r2r)
            v.tensor_copy(rowsb[:, 1, :], mrr)
            v.drain().then_inc(v_r, 1)
            v.wait_ge(t_st, 6)
            v.tensor_copy(r2b[:, 0:512], P[4][:])
            v.tensor_copy(r2b[:, 512:1024], P[5][:])
            v.tensor_copy(m2rb[:, 0:512], P[6][:])
            v.tensor_copy(m2rb[:, 512:1024], P[7][:])
            v.drain().then_inc(v_r, 1)
            for cidx in range(32):
                jj, rr = cidx // 4, cidx % 4
                v.wait_ge(d_c[cidx % 2], 16 * (cidx // 2 + 1))
                if jj >= 1:
                    v.wait_ge(t_z, 4 * (jj - 1) + rr + 1)
                v.tensor_tensor(xtmp[:], ybuf[:, cidx % 2, :], r2b[:], OP.mult)
                v.tensor_tensor(xn2[:, rr, :], xtmp[:], m2rb[:], OP.subtract)
                v.drain().then_inc(v_c1, 1)
                if rr == 3:
                    v.tensor_tensor(ybf[:, 0, :], xn2[:, 0, :], xn2[:, 1, :], OP.add)
                    v.tensor_tensor(ybf[:, 1, :], xn2[:, 2, :], xn2[:, 3, :], OP.add)
                    v.tensor_tensor(xf[:, jj, :], ybf[:, 0, :], ybf[:, 1, :], OP.add)
                    v.drain().then_inc(v_xf, 1)
            v.wait_ge(t_z, 32)
            v.tensor_single_scalar(z1bf[:, 0:512], P[4][0:2, :], 0.5, OP.mult)
            v.tensor_single_scalar(z1bf[:, 512:1024], P[5][0:2, :], 0.5, OP.mult)
            v.tensor_single_scalar(z3bf[:, 0:512], P[6][0:2, :], 0.5, OP.mult)
            v.tensor_single_scalar(z3bf[:, 512:1024], P[7][0:2, :], 0.5, OP.mult)
            v.drain().then_inc(v_z, 1)
            for hc in range(NHC):
                r_, a_ = hc // NCF, hc % NCF
                tgt0 = hf[:, a_, 0:512] if r_ == 0 else htmp[:, 0, :]
                tgt1 = hf[:, a_, 512:1024] if r_ == 0 else htmp[:, 1, :]
                v.wait_ge(s_si, 2 * hc + 1)
                v.wait_ge(t_ga, 4 * hc + 2)
                if hc >= 23:
                    v.wait_ge(t_z2, 2 * hc - 1)
                v.tensor_tensor(tgt0, silu01[:, 0, :], P[1][:], OP.mult)
                if r_ > 0:
                    v.tensor_tensor(hf[:, a_, 0:512], hf[:, a_, 0:512], htmp[:, 0, :],
                                    OP.add)
                v.drain().then_inc(v_h, 1)
                v.wait_ge(s_si, 2 * hc + 2)
                v.wait_ge(t_ga, 4 * hc + 4)
                if hc >= 23:
                    v.wait_ge(t_z2, 2 * hc)
                v.tensor_tensor(tgt1, silu01[:, 1, :], P[3][:], OP.mult)
                if r_ > 0:
                    v.tensor_tensor(hf[:, a_, 512:1024], hf[:, a_, 512:1024],
                                    htmp[:, 1, :], OP.add)
                v.drain().then_inc(v_h, 1)
            v.wait_ge(t_z2, 2 * NHC)
            v.tensor_single_scalar(z2bf[:, 0:512], P[4][0:2, :], 0.5, OP.mult)
            v.tensor_single_scalar(z2bf[:, 512:1024], P[5][0:2, :], 0.5, OP.mult)
            v.drain().then_inc(v_z2, 1)
            for dc in range(32):
                pa, pb = (P[0], P[1]) if dc % 2 == 0 else (P[2], P[3])
                v.wait_ge(t_dn, dc + 1)
                v.wait_ge(d_c[dc % 2], 16 * (16 + dc // 2 + 1))
                if dc >= 2:
                    v.wait_ge(d_o[dc % 2], 16 * (dc // 2))
                v.tensor_tensor(ostg[:, dc % 2, 0:512], pa[:], ybuf[:, dc % 2, 0:512],
                                OP.add)
                v.tensor_tensor(ostg[:, dc % 2, 512:1024], pb[:],
                                ybuf[:, dc % 2, 512:1024], OP.add)
                v.drain().then_inc(v_dn, 1)

        @block.scalar
        def _(s):
            s.wait_ge(d_pro, 16 * 3)
            s.activation(statc[:, 15:16], identt[:, 1:2], AF.Square)
            s.activation(statc[:, 15:16], identt[:, 0:1], AF.Exp)
            s.activation(statc[:, 15:16], identt[:, 0:1], AF.Sigmoid)
            s.activation(statc[:, 15:16], identt[:, 0:1], AF.Silu)
            s.activation(statc[:, 15:16], identt[:, 0:1], AF.Exp)
            for i in range(32):
                s.wait_ge(d_a[i % 2], 16 * (i // 2 + 1))
                sqc = statc[:, 6 + i % 2:7 + i % 2]
                vpec = statc[:, 8 + i % 2:9 + i % 2]
                s.activation(trashS[:], xbuf[i % 2][:], AF.Square, accum_out=sqc)
                s.drain().then_inc(s_a, 1)
                s.wait_ge(v_a, 2 * i + 1)
                s.activation(statc[:, 11:12], vpec, AF.Ln)
                s.activation(statc[:, 10:11], statc[:, 11:12], AF.Exp, scale=-0.5)
                s.drain().then_inc(s_a, 1)
            for h in range(H):
                for G in range(2):
                    for st in range(SGRP[G]):
                        sx = 48 * h + 16 * G + st
                        s.wait_ge(t_sc, sx + 1)
                        if sx - 1 > 0:
                            s.wait_ge(v_str, sx - 1)
                        s.activation(praw[:, sx % 2, :], P[sx % 2][:], AF.Sigmoid)
                        s.drain().then_inc(s_sig, 1)
            s.wait_ge(v_r, 1)
            s.activation(rows[:, 3, :], rows[:, 2, :], AF.Ln)
            s.activation(rows[:, 3, :], rows[:, 3, :], AF.Exp, scale=-0.5)
            s.drain().then_inc(s_c0, 1)
            for hc in range(NHC):
                s.wait_ge(t_ga, 4 * hc + 1)
                if hc >= 1:
                    s.wait_ge(v_h, 2 * hc - 1)
                s.activation(silu01[:, 0, :], P[0][:], AF.Silu)
                s.drain().then_inc(s_si, 1)
                s.wait_ge(t_ga, 4 * hc + 3)
                s.wait_ge(v_h, 2 * hc)
                s.activation(silu01[:, 1, :], P[2][:], AF.Silu)
                s.drain().then_inc(s_si, 1)

        @block.tensor
        def _(t):
            t.wait_ge(v_pro, 1)
            for dc in range(8):
                if dc >= 1:
                    t.wait_ge(v_pro, 1 + dc)
                t.matmul(P[6][0:128, 0:4], qk4[0:4, 128 * dc:128 * (dc + 1)],
                         identt[0:4, 0:4], start=True, stop=True)
                t.matmul(P[7][0:128, 0:4], vv4[0:4, 128 * dc:128 * (dc + 1)],
                         identt[0:4, 0:4], start=True, stop=True).then_inc(t_pro, 1)
            for h in range(H):
                if h == 0:
                    t.wait_ge(v_q, 1)
                    t.wait_ge(d_k2, 128)
                else:
                    t.wait_ge(v_q, h + 2)
                    t.wait_ge(d_k, 128 * (h + 1))
                for G in range(2):
                    if h == 0 and G == 1:
                        t.wait_ge(v_q, 2)
                        t.wait_ge(d_k, 128)
                    SG = SGRP[G]
                    base = 48 * h + 16 * G
                    for st in range(SG):
                        sx = base + st
                        if sx - 1 > 0:
                            t.wait_ge(s_sig, sx - 1)
                        for j in range(8):
                            mm = t.matmul(P[sx % 2][:],
                                          kfm[:, j, 128 * st:128 * (st + 1)],
                                          qh[:, j, 512 * G:512 * (G + 1)],
                                          start=(j == 0), stop=(j == 7))
                            if j == 7:
                                mm.then_inc(t_sc, 1)
                    t.wait_ge(v_str, base + SG)
                    if h == 0:
                        t.wait_ge(d_v2 if G == 0 else d_v, 32)
                    else:
                        t.wait_ge(d_v, 32 + 64 * h)
                    for p in range(2):
                        pi = 4 * h + 2 * G + p
                        if pi >= 1:
                            pprev = pi - 1
                            t.wait_ge(v_sq, pprev + 1)
                            gw = g0w if (pprev // 2) % 2 == 0 else g1w
                            for j in range(4):
                                t.matmul(P[6][0:33, :], gw[:], castb[:, j, :],
                                         start=(pprev == 0 and j == 0),
                                         stop=False, skip_group_check=True)
                            for j in range(4):
                                t.matmul(P[7][0:33, :], gw[:], sqb[:, j, :],
                                         start=(pprev == 0 and j == 0),
                                         stop=False, skip_group_check=True)
                        if 4 * pi > 0:
                            t.wait_ge(v_ev, 4 * pi)
                        for j in range(4):
                            dsl = 4 * p + j
                            for st in range(SG):
                                mm = t.matmul(P[2 + j][:],
                                              vtm[:, st, 128 * dsl:128 * (dsl + 1)],
                                              strip[:, st, :],
                                              start=(st == 0), stop=(st == SG - 1),
                                              skip_group_check=True)
                                if st == SG - 1 and j == 3:
                                    mm.then_inc(t_pv, 1)
                                elif G == 1 and p == 1 and j == 3 and st % 8 == 7:
                                    mm.then_inc(t_vf, 1)
                        if p == 1 and G == 1:
                            t.nop().then_inc(t_vf, 1)
            t.wait_ge(v_sq, 16)
            for j in range(4):
                mm = t.matmul(P[6][0:33, :], g1w[:], castb[:, j, :],
                              start=False, stop=(j == 3), skip_group_check=True)
                if j == 3:
                    mm.then_inc(t_st, 1)
            for j in range(4):
                mm = t.matmul(P[7][0:33, :], g1w[:], sqb[:, j, :],
                              start=False, stop=(j == 3), skip_group_check=True)
                if j == 3:
                    mm.then_inc(t_st, 1)
            t.wait_ge(v_r, 2)
            t.matmul(P[4][:], onesrow[0:1, :], rowsb[0:1, 0, 0:512],
                     start=True, stop=True).then_inc(t_st, 1)
            t.matmul(P[5][:], onesrow[0:1, :], rowsb[0:1, 0, 512:1024],
                     start=True, stop=True).then_inc(t_st, 1)
            t.matmul(P[6][:], onesrow[0:1, :], rowsb[0:1, 1, 0:512],
                     start=True, stop=True).then_inc(t_st, 1)
            t.matmul(P[7][:], onesrow[0:1, :], rowsb[0:1, 1, 512:1024],
                     start=True, stop=True).then_inc(t_st, 1)
            t.wait_ge(d_wa, 32)
            t.wait_ge(v_r, 3)
            for cidx in range(32):
                jj, rr = cidx // 4, cidx % 4
                c = jj + 8 * rr
                t.wait_ge(v_c1, cidx + 1)
                st_, sp_ = (cidx == 0), (cidx == 31)
                t.matmul(P[4][0:2, :], a13sb[:, c, 0:2], xn2[:, rr, 0:512],
                         start=st_, stop=sp_, skip_group_check=True)
                t.matmul(P[5][0:2, :], a13sb[:, c, 0:2], xn2[:, rr, 512:1024],
                         start=st_, stop=sp_, skip_group_check=True)
                t.matmul(P[6][0:2, :], a13sb[:, c, 2:4], xn2[:, rr, 0:512],
                         start=st_, stop=sp_, skip_group_check=True)
                t.matmul(P[7][0:2, :], a13sb[:, c, 2:4], xn2[:, rr, 512:1024],
                         start=st_, stop=sp_, skip_group_check=True).then_inc(t_z, 1)
            t.wait_ge(v_xf, 8)
            t.wait_ge(v_z, 1)
            t.wait_ge(d_wa, 80)

            def z2rhs(hcp, half):
                r_, a_ = hcp // NCF, hcp % NCF
                if r_ == 0:
                    return hf[:, a_, 512 * half:512 * (half + 1)]
                return htmp[:, half, :]

            for hc in range(NHC):
                s_ = hc // 4
                wi = hc % 4
                if wi == 0:
                    t.wait_ge(d_ws[s_ % 2], 32 * (s_ // 2 + 1))
                if hc >= 1:
                    t.wait_ge(v_h, 2 * (hc - 1) + 1)
                    t.matmul(P[4][0:2, :], a2sb[:, hc - 1, :], z2rhs(hc - 1, 0),
                             start=(hc - 1 == 0), stop=False, skip_group_check=True
                             ).then_inc(t_z2, 1)
                    t.wait_ge(s_si, 2 * (hc - 1) + 1)
                for cch in range(8):
                    t.matmul(P[0][:], wsl1[:, s_ % 2, cch, 128 * wi:128 * (wi + 1)],
                             xf[:, cch, 0:512], start=(cch == 0), stop=False,
                             skip_group_check=True)
                t.matmul(P[0][:], b1aug[0:2, 128 * hc:128 * (hc + 1)], z1bf[0:2, 0:512],
                         start=False, stop=True, skip_group_check=True
                         ).then_inc(t_ga, 1)
                for cch in range(8):
                    t.matmul(P[1][:], wsl3[:, s_ % 2, cch, 128 * wi:128 * (wi + 1)],
                             xf[:, cch, 0:512], start=(cch == 0), stop=False,
                             skip_group_check=True)
                t.matmul(P[1][:], b3aug[0:2, 128 * hc:128 * (hc + 1)], z3bf[0:2, 0:512],
                         start=False, stop=True, skip_group_check=True
                         ).then_inc(t_ga, 1)
                if hc >= 1:
                    t.wait_ge(v_h, 2 * (hc - 1) + 2)
                    t.matmul(P[5][0:2, :], a2sb[:, hc - 1, :], z2rhs(hc - 1, 1),
                             start=(hc - 1 == 0), stop=False, skip_group_check=True
                             ).then_inc(t_z2, 1)
                    t.wait_ge(s_si, 2 * (hc - 1) + 2)
                    t.wait_ge(v_h, 2 * hc)
                for cch in range(8):
                    t.matmul(P[2][:], wsl1[:, s_ % 2, cch, 128 * wi:128 * (wi + 1)],
                             xf[:, cch, 512:1024], start=(cch == 0), stop=False,
                             skip_group_check=True)
                t.matmul(P[2][:], b1aug[0:2, 128 * hc:128 * (hc + 1)],
                         z1bf[0:2, 512:1024], start=False, stop=True,
                         skip_group_check=True).then_inc(t_ga, 1)
                for cch in range(8):
                    t.matmul(P[3][:], wsl3[:, s_ % 2, cch, 128 * wi:128 * (wi + 1)],
                             xf[:, cch, 512:1024], start=(cch == 0), stop=False,
                             skip_group_check=True)
                t.matmul(P[3][:], b3aug[0:2, 128 * hc:128 * (hc + 1)],
                         z3bf[0:2, 512:1024], start=False, stop=True,
                         skip_group_check=True).then_inc(t_ga, 1)
            t.wait_ge(v_h, 2 * NHC - 1)
            t.matmul(P[4][0:2, :], a2sb[:, NHC - 1, :], z2rhs(NHC - 1, 0),
                     start=False, stop=True, skip_group_check=True).then_inc(t_z2, 1)
            t.wait_ge(v_h, 2 * NHC)
            t.matmul(P[5][0:2, :], a2sb[:, NHC - 1, :], z2rhs(NHC - 1, 1),
                     start=False, stop=True, skip_group_check=True).then_inc(t_z2, 1)
            t.wait_ge(v_z2, 1)
            t.wait_ge(s_si, 2 * NHC)
            for dc in range(32):
                sd, wi2 = dc // 4, dc % 4
                pa, pb = (P[0], P[1]) if dc % 2 == 0 else (P[2], P[3])
                if wi2 == 0:
                    t.wait_ge(d_w2[sd % 2], 16 * (sd // 2 + 1))
                if dc >= 2:
                    t.wait_ge(v_dn, dc - 1)
                for cf in range(NCF):
                    t.matmul(pa[:], w2sl[:, sd % 2, cf, 128 * wi2:128 * (wi2 + 1)],
                             hf[:, cf, 0:512], start=(cf == 0), stop=False,
                             skip_group_check=True)
                t.matmul(pa[:], b2aug[0:2, 128 * dc:128 * (dc + 1)], z2bf[0:2, 0:512],
                         start=False, stop=True, skip_group_check=True)
                for cf in range(NCF):
                    t.matmul(pb[:], w2sl[:, sd % 2, cf, 128 * wi2:128 * (wi2 + 1)],
                             hf[:, cf, 512:1024], start=(cf == 0), stop=False,
                             skip_group_check=True)
                t.matmul(pb[:], b2aug[0:2, 128 * dc:128 * (dc + 1)], z2bf[0:2, 512:1024],
                         start=False, stop=True, skip_group_check=True
                         ).then_inc(t_dn, 1)

    return nc


_NC_CACHE = {}


def _get_nc():
    if "nc" not in _NC_CACHE:
        _NC_CACHE["nc"] = build_nc()
    return _NC_CACHE["nc"]


def _pad_hidden(a, axis):
    sh = list(a.shape)
    assert sh[axis] == 11008
    a2 = np.moveaxis(np.asarray(a, np.float32), axis, 0)
    rest = a2.shape[1:]
    src = a2.reshape(4, 2752, *rest)
    dst = np.zeros((4, 2816, *rest), np.float32)
    dst[:, :2752] = src
    out = dst.reshape(HIDP, *rest)
    return np.moveaxis(out, 0, axis)


def _prep_consts(bv_q, bv_k, bv_v, w1q, a1, b1, w3q, a3, b3, w2q, a2, b2):
    f = np.float32
    c = {}
    c["iota"] = np.ascontiguousarray(
        np.broadcast_to(np.arange(512, dtype=f), (128, 512)))
    c["ident"] = np.eye(128, dtype=f)
    c["bvq"] = np.ascontiguousarray(np.asarray(bv_q, f))
    c["bvk"] = np.ascontiguousarray(np.asarray(bv_k, f))
    c["bvv"] = np.ascontiguousarray(np.asarray(bv_v, f))

    def prep_w13(wq):
        wT = np.asarray(wq, f).T
        wTp = _pad_hidden(wT, 1)
        a = wTp.reshape(8, 128, HIDP).transpose(1, 0, 2)
        a = a.reshape(128, 8, NSLAB, 512).transpose(0, 2, 1, 3)
        return np.ascontiguousarray(a)

    c["w1p"] = prep_w13(w1q)
    c["w3p"] = prep_w13(w3q)
    w2T = np.asarray(w2q, f).T
    w2Tp = np.zeros((CFP, D), f)
    w2Tp[:2752] = w2T
    a = w2Tp.reshape(NCF, 128, D).transpose(1, 0, 2)
    a = a.reshape(128, NCF, NW2SL, 512).transpose(0, 2, 1, 3)
    c["w2p"] = np.ascontiguousarray(a)
    a13 = np.concatenate([np.asarray(a1, f), np.asarray(a3, f)], 0)
    c["a13p"] = np.ascontiguousarray(a13.T.reshape(32, 128, 4).transpose(1, 0, 2))
    a2p_ = _pad_hidden(np.asarray(a2, f), 1)
    c["a2p"] = np.ascontiguousarray(a2p_.T.reshape(NHC, 128, 2).transpose(1, 0, 2))
    c["b1T"] = np.ascontiguousarray(_pad_hidden(np.asarray(b1, f), 0).T)
    c["b3T"] = np.ascontiguousarray(_pad_hidden(np.asarray(b3, f), 0).T)
    c["b2T"] = np.ascontiguousarray(np.asarray(b2, f).T)
    return c


def _core_layout(k):
    chunks = [4 * i + k for i in range(4)]
    crow = [np.arange(256 * j, 256 * j + 256) for j in chunks]
    inchunk = np.zeros(T, bool)
    for r in crow:
        inchunk[r] = True
    nonchunk = np.nonzero(~inchunk)[0]
    parts = []
    for i in range(4):
        parts.append(crow[i])
        parts.append(nonchunk[768 * i:768 * (i + 1)])
    perm = np.concatenate(parts)
    qrows = np.concatenate(crow)
    return perm, qrows, chunks


def _core_inputs(consts, x, c):
    b, k = c // 4, c % 4
    perm, qrows, chunks = _core_layout(k)
    m = dict(consts)
    m["x"] = np.ascontiguousarray(x[b][perm])
    m["xqT"] = np.ascontiguousarray(x[b][qrows].T)
    thr_v = np.zeros((128, 96), np.float32)
    p = np.arange(128)
    for G in range(2):
        for st in range(SGRP[G]):
            gkey = perm[128 * st + p].astype(np.float32)
            for half in range(2):
                slot = 2 * G + half
                thr_v[:, 2 * (16 * G + st) + half] = gkey - 256 * chunks[slot]
    m["thr"] = np.ascontiguousarray(thr_v)
    return m


def kernel(L0_attn_x, L0_attn_bv_q, L0_attn_bv_k, L0_attn_bv_v, L0_ffn_w1q, L0_ffn_a1,
           L0_ffn_b1, L0_ffn_w3q, L0_ffn_a3, L0_ffn_b3, L0_ffn_w2q, L0_ffn_a2,
           L0_ffn_b2):
    x = np.asarray(L0_attn_x, np.float32)
    B = x.shape[0]
    consts = _prep_consts(L0_attn_bv_q, L0_attn_bv_k, L0_attn_bv_v, L0_ffn_w1q,
                          L0_ffn_a1, L0_ffn_b1, L0_ffn_w3q, L0_ffn_a3, L0_ffn_b3,
                          L0_ffn_w2q, L0_ffn_a2, L0_ffn_b2)
    in_maps = [_core_inputs(consts, x, c) for c in range(8)]
    nc = _get_nc()
    res = run_bass_kernel_spmd(nc, in_maps, core_ids=list(range(8)))
    out = np.empty((B, T, D), np.float32)
    for c in range(8):
        b, k = c // 4, c % 4
        _, qrows, _ = _core_layout(k)
        out[b, qrows, :] = res.results[c]["out"].T
    return out
